# version 02 (final)
# baseline (speedup 1.0000x reference)
import numpy as np
import ml_dtypes
import concourse.bass as bass
import concourse.mybir as mybir
from concourse.bass_utils import run_bass_kernel_spmd

F32 = mybir.dt.float32
BF16 = mybir.dt.bfloat16
AF = mybir.ActivationFunctionType
ALU = mybir.AluOpType
AX = mybir.AxisListType.X
BF16NP = ml_dtypes.bfloat16

B = 16
LC = 256
D = 2048
NT = B * LC
NCORES = 8
NH = 16


class Ctx:
    def __init__(self, nc, block, sems):
        self.nc = nc
        self.block = block
        self.sem = sems
        self.cnt = {k: 0 for k in sems}
        self.marks = {}

    def inc(self, instr, name, amt=1):
        self.cnt[name] += amt
        instr.then_inc(self.sem[name], amt)
        return self.cnt[name]

    def wait(self, eng, name, val=None):
        v = self.cnt[name] if val is None else val
        if v > 0:
            eng.wait_ge(self.sem[name], v)

    def mwait(self, eng, name, key, default=0):
        v = self.marks.get(key, default)
        if v > 0:
            eng.wait_ge(self.sem[name], v)


SEM_NAMES = ["dC", "dXB", "dXF", "dP0", "dP1", "dWO", "dX0", "dRS", "dLR",
             "dLS", "tQ", "vE", "aE", "gE"]


def build_program(phases=("A",), external_mid=True, ntiles=4, dump_r=False):
    nc = bass.Bass("TRN2", target_bir_lowering=False,
                   detect_race_conditions=False)

    def dram(name, shape, dtype, kind):
        return nc.dram_tensor(name, shape, dtype, kind=kind).ap()

    EIN, EOUT, INT = "ExternalInput", "ExternalOutput", "Internal"
    io = {}
    io["x0f"] = dram("x0f", [NT, D], F32, EIN)
    io["x0b"] = dram("x0b", [NT, D], BF16, INT)
    io["c_ones_s"] = dram("c_ones_s", [1, 128], F32, EIN)
    io["c_ones"] = dram("c_ones", [1, 128], F32, EIN)
    io["c_triu4"] = dram("c_triu4", [128, 256], F32, EIN)
    io["c_ident"] = dram("c_ident", [128, 128], BF16, EIN)
    io["c_identf"] = dram("c_identf", [128, 128], F32, EIN)
    io["c_epsw"] = dram("c_epsw", [128, 8], F32, EIN)

    if "A" in phases:
        for w in ["wq_cp", "wk_cp", "wv_cp", "wog_cp", "woT"]:
            io[w] = dram(w, [D, D], F32, EIN)
        io["bwT_r"] = dram("bwT_r", [128, 256], F32, EIN)
        io["bb_r"] = dram("bb_r", [1, 16], F32, EIN)
        for b in ["qb_r", "kb_r", "vb_r"]:
            io[b] = dram(b, [128, 16], F32, EIN)
        io["obrow"] = dram("obrow", [1, D], F32, EIN)
        io["r1D"] = dram("r1D", [NT, D], F32, EOUT if dump_r else INT)
        io["dbgT"] = dram("dbgT", [NT, 8], F32, EOUT if dump_r else INT)
        k1 = EOUT if external_mid else INT
        io["xln1f"] = dram("xln1f", [NT, D], F32, k1)
        io["xln1b"] = dram("xln1b", [NT, D], BF16, k1)

    if "B" in phases:
        io["sw_r"] = dram("sw_r", [D, 64], F32, EIN)
        io["lu_r"] = dram("lu_r", [D, 64], F32, EIN)
        io["mu_r"] = dram("mu_r", [D, 8], F32, EIN)
        io["deb_r"] = dram("deb_r", [D, 8], F32, EIN)
        io["ldg_r"] = dram("ldg_r", [64, D], F32, EIN)
        io["bw8"] = dram("bw8", [8, D], F32, EIN)
        io["cen8"] = dram("cen8", [8, 1], F32, EIN)
        io["slp8"] = dram("slp8", [8, 1], F32, EIN)
        io["c_cmask"] = dram("c_cmask", [64, 8], F32, EIN)
        io["c_i8"] = dram("c_i8", [8, 8], F32, EIN)
        io["c_bo8"] = dram("c_bo8", [64, 8], F32, EIN)
        io["c_e8"] = dram("c_e8", [8, 64], BF16, EIN)
        io["r2D"] = dram("r2D", [NT, D], F32, INT)
        k2 = EOUT if external_mid else INT
        io["xln2f"] = dram("xln2f", [NT, D], F32, k2)
        io["xln2b"] = dram("xln2b", [NT, D], BF16, k2)
        io["xln2c"] = dram("xln2c", [NT, D], BF16, k2)
        if "A" not in phases:
            io["xln1f"] = dram("xln1f", [NT, D], F32, EIN)
            io["xln1b"] = dram("xln1b", [NT, D], BF16, EIN)

    if "C" in phases:
        io["wqc_cp"] = dram("wqc_cp", [D, D], F32, EIN)
        io["wkvc_cp"] = dram("wkvc_cp", [512, D], F32, EIN)
        io["woc"] = dram("woc", [D, D], F32, EIN)
        io["qbc_r"] = dram("qbc_r", [128, 16], F32, EIN)
        io["kvbc_r"] = dram("kvbc_r", [128, 4], F32, EIN)
        io["obc_row"] = dram("obc_row", [1, D], F32, EIN)
        io["r3D"] = dram("r3D", [NT, D], F32, EOUT if dump_r else INT)
        k3 = EOUT if external_mid else INT
        io["xln3f"] = dram("xln3f", [NT, D], F32, k3)
        io["xln3b"] = dram("xln3b", [NT, D], BF16, k3)
        if "B" not in phases:
            io["xln2f"] = dram("xln2f", [NT, D], F32, EIN)
            io["xln2c"] = dram("xln2c", [NT, D], BF16, EIN)

    if "D" in phases:
        io["wup_cp"] = dram("wup_cp", [8192, D], F32, EIN)
        io["wgate_cp"] = dram("wgate_cp", [8192, D], F32, EIN)
        io["wdownT"] = dram("wdownT", [8192, D], F32, EIN)
        io["wsu_r"] = dram("wsu_r", [128, 64], F32, EIN)
        io["wsg_r"] = dram("wsg_r", [128, 64], F32, EIN)
        io["sudp_r"] = dram("sudp_r", [128, 64], F32, EIN)
        io["suug_r"] = dram("suug_r", [128, 16], F32, EIN)
        io["wsdp_row"] = dram("wsdp_row", [1, D], F32, EIN)
        io["c_h128"] = dram("c_h128", [128, 128], BF16, EIN)
        io["c_h64"] = dram("c_h64", [64, 64], BF16, EIN)
        io["wupb"] = dram("wupb", [8192, D], BF16, INT)
        io["wgateb"] = dram("wgateb", [8192, D], BF16, INT)
        io["w2db"] = dram("w2db", [8192, D], BF16, INT)
        io["r4D"] = dram("r4D", [NT, D], F32, INT)
        io["xout"] = dram("xout", [NT, D], F32, EOUT)
        if "C" not in phases:
            io["xln3f"] = dram("xln3f", [NT, D], F32, EIN)
            io["xln3b"] = dram("xln3b", [NT, D], BF16, EIN)

    import contextlib
    stack = contextlib.ExitStack()
    sems = {s: stack.enter_context(nc.semaphore(s)) for s in SEM_NAMES}
    block = stack.enter_context(nc.Block())
    ctx = Ctx(nc, block, sems)

    import os as _os
    if "A" in phases:
        emit_phase_a(ctx, io, ntiles=ntiles, stage=_os.environ.get("KDEV_STAGE", "full"))
    if "B" in phases:
        emit_phase_b(ctx, io, ntiles=ntiles)
    if "C" in phases:
        emit_phase_c(ctx, io)
    if "D" in phases:
        emit_phase_d(ctx, io)

    @block.sync
    def _(s):
        for nm in ["dRS", "dLS"]:
            ctx.wait(s, nm)

    stack.close()
    return nc, io


def emit_phase_a(ctx, io, ntiles=4, stage='full'):
    nc, block = ctx.nc, ctx.block
    TA = 1024
    NTC = 8

    al = nc.alloc_sbuf_tensor
    XF = al("a_xf", [128, 16 * TA], BF16)
    GV = al("a_gv", [128, NTC * D], BF16)
    OUTG = al("a_outg", [128, NH * TA], BF16)
    WP = [al(f"a_wp{i}", [128, 4 * D], BF16) for i in range(2)]
    WOS = al("a_wo0", [128, 16 * 512], BF16)
    QF = al("a_qf", [128, TA], F32)
    KF = al("a_kf", [128, TA], F32)
    SQ = al("a_sq", [128, TA], F32)
    GT = al("a_gt", [128, TA], F32)
    QB_ = al("a_qbf", [128, TA], BF16)
    KB_ = al("a_kbf", [128, TA], BF16)
    VB_ = al("a_vbf", [128, TA], BF16)
    NRMQ = al("a_nrmq", [1, TA], F32)
    NRMK = al("a_nrmk", [1, TA], F32)
    ST_ = al("a_st", [128, 256], BF16)
    BTA = al("a_beta", [128, NTC * NH], F32)
    BSC = al("a_bsc", [128, NTC * NH], F32)
    BWS = al("a_bws", [128, 256], BF16)
    BBS = al("a_bbs", [128, 16], F32)
    QBS = al("a_qbs", [128, 16], F32)
    KBS = al("a_kbs", [128, 16], F32)
    VBS = al("a_vbs", [128, 16], F32)
    OBB = al("a_obb", [128, D], F32)
    ONS = al("a_ons", [1, 128], F32)
    ONE = al("a_one", [1, 128], F32)
    TRIU = al("a_triu", [128, 256], F32)
    EQ1 = al("a_eq1", [128, 2], F32)
    IDN = al("a_idn", [128, 128], BF16)
    X0S = al("a_x0s0", [128, 512], F32)
    RSL = al("a_rsl0", [128, 512], F32)
    R1 = al("a_r1_0", [128, D], F32)
    XLB = al("a_xlb", [128, D], BF16)
    STT = al("a_stt", [128, 8], F32)
    EPS = al("a_eps", [128, 1], F32)
    BBR = al("a_bbr", [1, 16], F32)
    OBR = al("a_obr", [1, 512], F32)
    ONE1 = al("a_one1", [128, 1], F32)

    P = [nc.alloc_psum_tensor(f"a_p{i}", [128, 512], F32) for i in range(6)]
    PS7 = nc.alloc_psum_tensor("a_p7", [128, 512], F32)
    PT = nc.alloc_psum_tensor("a_pt", [128, 128], BF16)

    sb = dict(R1=R1, XLB=XLB, STT=STT, EPS=EPS)

    @block.sync
    def _(s):
        for dst, src in [(ONS, "c_ones_s"), (ONE, "c_ones"), (TRIU, "c_triu4"),
                         (QBS, "qb_r"), (KBS, "kb_r"), (VBS, "vb_r")]:
            ctx.inc(s.dma_start(out=dst[:], in_=io[src]), "dC", 16)
        ctx.inc(s.dma_start(out=BBR[:], in_=io["bb_r"]), "dC", 16)
        ctx.inc(s.dma_start(out=IDN[:], in_=io["c_ident"]), "dC", 16)

    @block.gpsimd
    def _(g):
        ctx.inc(g.dma_start(out=BWS[:], in_=io["bwT_r"]), "dXB", 16)
        for i in range(4):
            ctx.inc(g.dma_start(out=io["x0b"][i * 1024:(i + 1) * 1024, :],
                                in_=io["x0f"][i * 1024:(i + 1) * 1024, :]), "dXB", 16)
    ctx.marks["x0b"] = ctx.cnt["dXB"]

    @block.vector
    def _(v):
        v.memset(EPS[:], 1e-5)
        v.memset(EQ1[:], 1.0)
        ctx.inc(v.memset(ONE1[:], 1.0), "vE")

    @block.tensor
    def _(t):
        ctx.wait(t, "dC")
        ctx.inc(t.matmul(P[4][:, 0:16], ONE[0:1, :], BBR[0:1, :],
                         start=True, stop=True), "tQ")

    @block.vector
    def _(v):
        ctx.wait(v, "tQ")
        ctx.inc(v.tensor_copy(BBS[:], P[4][:, 0:16]), "vE")

    for os in range(4):
        @block.sync
        def _(s, os=os):
            ctx.wait(s, "dC")
            if os > 0:
                ctx.wait(s, "tQ")
            ctx.inc(s.dma_start(out=OBR[:],
                                in_=io["obrow"][0:1, os * 512:(os + 1) * 512]), "dC", 16)

        @block.tensor
        def _(t, os=os):
            ctx.wait(t, "dC")
            ctx.wait(t, "vE")
            ctx.inc(t.matmul(P[4][:, 0:512], ONE[0:1, :], OBR[0:1, 0:512],
                             start=True, stop=True), "tQ")

        @block.vector
        def _(v, os=os):
            ctx.wait(v, "tQ")
            ctx.inc(v.tensor_copy(OBB[:, os * 512:(os + 1) * 512], P[4][:, 0:512]), "vE")

    for ti in range(ntiles):
        _emit_a_tile(ctx, io, ti, TA, NTC, XF, GV, OUTG, WP, WOS, QF, KF, SQ, GT,
                     QB_, KB_, VB_, NRMQ, NRMK, ST_, BTA, BSC, BWS, BBS, QBS, KBS,
                     VBS, OBB, ONS, ONE, TRIU, EQ1, IDN, X0S, RSL, P, PT, PS7, ONE1,
                     stage=stage)
        if stage == 'full':
            _emit_lnf(ctx, sb, io["r1D"], io["xln1f"], io["xln1b"],
                      [ti * TA + k * 128 for k in range(NTC)], dbg=io.get("dbgT"))


def _emit_a_tile(ctx, io, ti, TA, NTC, XF, GV, OUTG, WP, WOS, QF, KF, SQ, GT,
                 QB_, KB_, VB_, NRMQ, NRMK, ST_, BTA, BSC, BWS, BBS, QBS, KBS,
                 VBS, OBB, ONS, ONE, TRIU, EQ1, IDN, X0S, RSL, P, PT, PS7, ONE1,
                 stage='full'):
    nc, block = ctx.nc, ctx.block
    t0 = ti * TA
    M = ctx.marks

    @block.sync
    def _(s):
        ctx.wait(s, "dXF")
        s.wait_ge(ctx.sem["dXB"], M["x0b"])
        if ti > 0:
            ctx.wait(s, "tQ")
        for c in range(16):
            ctx.inc(s.dma_start_transpose(
                XF[:, c * TA:(c + 1) * TA],
                io["x0b"][t0:t0 + TA, c * 128:(c + 1) * 128]), "dXF", 16)
    xf_cnt = ctx.cnt["dXF"]

    for tc in range(NTC):
        @block.tensor
        def _(t, tc=tc):
            if tc == 0:
                t.wait_ge(ctx.sem["dXF"], xf_cnt)
                ctx.wait(t, "dXB")
            ctx.wait(t, "vE")
            for kc in range(16):
                mm = t.matmul(P[4][:, 0:16],
                              XF[:, kc * TA + tc * 128: kc * TA + tc * 128 + 128],
                              BWS[:, kc * 16:(kc + 1) * 16],
                              start=(kc == 0), stop=(kc == 15))
            ctx.inc(mm, "tQ")

        @block.vector
        def _(v, tc=tc):
            ctx.wait(v, "tQ")
            ctx.inc(v.tensor_add(BSC[:, tc * NH:tc * NH + 16], P[4][:, 0:16], BBS[:]), "vE")

    @block.scalar
    def _(a):
        ctx.wait(a, "vE")
        a.activation(BTA[:], BSC[:], AF.Abs)
        a.activation(BTA[:], BTA[:], AF.Exp, scale=-1.0)
        ctx.inc(a.activation(BTA[:], BTA[:], AF.Ln, bias=ONE1[:, 0:1]), "aE")

    @block.vector
    def _(v):
        ctx.wait(v, "aE")
        v.tensor_scalar(BSC[:], BSC[:], 0.0, None, ALU.max)
        ctx.inc(v.tensor_add(BTA[:], BTA[:], BSC[:]), "vE")
    beta_ready = ctx.cnt["vE"]
    if stage == 'beta':
        return

    nheads = 1 if stage == 'heads1' else NH
    for hc in range(nheads):
        slot = hc % 2
        psem = "dP0" if slot == 0 else "dP1"

        @block.gpsimd
        def _(g, hc=hc, slot=slot, psem=psem):
            ctx.wait(g, psem)
            ctx.mwait(g, "tQ", ("panelfree", slot))
            for wi, wname in enumerate(["wq_cp", "wk_cp", "wv_cp", "wog_cp"]):
                ctx.inc(g.dma_start(out=WP[slot][:, wi * D:(wi + 1) * D],
                                    in_=io[wname][hc * 128:(hc + 1) * 128, :]), psem, 16)
        panel_cnt = ctx.cnt[psem]
        if stage == 'panels':
            @block.tensor
            def _(t, panel_cnt=panel_cnt, psem=psem):
                t.wait_ge(ctx.sem[psem], panel_cnt)
            continue

        def proj(dst_psums, woff, wait_aE=0, wait_vE=0, slot=slot, psem=psem,
                 panel_cnt=panel_cnt):
            @block.tensor
            def _(t):
                t.wait_ge(ctx.sem[psem], panel_cnt)
                if wait_aE:
                    t.wait_ge(ctx.sem["aE"], wait_aE)
                if wait_vE:
                    t.wait_ge(ctx.sem["vE"], wait_vE)
                for h in range(2):
                    for kc in range(16):
                        mm = t.matmul(dst_psums[h][:, 0:512],
                                      WP[slot][:, woff * D + kc * 128: woff * D + kc * 128 + 128],
                                      XF[:, kc * TA + h * 512: kc * TA + h * 512 + 512],
                                      start=(kc == 0), stop=(kc == 15))
                ctx.inc(mm, "tQ")

        proj([P[0], P[1]], 2, wait_aE=M.get("p01_aE", 0))
        @block.scalar
        def _(a, hc=hc):
            ctx.wait(a, "tQ")
            for h in range(2):
                act = a.activation(VB_[:, h * 512:(h + 1) * 512], P[h][:, 0:512],
                                   AF.Identity, bias=VBS[:, hc:hc + 1])
            ctx.inc(act, "aE")
        vb_ready = ctx.cnt["aE"]

        for tc in range(NTC):
            @block.tensor
            def _(t, tc=tc):
                if tc == 0:
                    t.wait_ge(ctx.sem["aE"], vb_ready)
                    ctx.mwait(t, "vE", ("ptfree",))
                else:
                    ctx.wait(t, "vE")
                ctx.inc(t.transpose(PT[:], VB_[:, tc * 128:(tc + 1) * 128], IDN[:]), "tQ")

            @block.vector
            def _(v, tc=tc, hc=hc):
                ctx.wait(v, "tQ")
                if tc == 0:
                    v.wait_ge(ctx.sem["vE"], beta_ready)
                ctx.inc(v.tensor_scalar(
                    GV[:, tc * D + hc * 128: tc * D + hc * 128 + 128],
                    PT[:], BTA[:, tc * NH + hc: tc * NH + hc + 1], None, ALU.mult), "vE")
        M[("ptfree",)] = ctx.cnt["vE"]
        if stage == 'vproj':
            continue

        proj([P[0], P[1]], 0, wait_aE=vb_ready)
        @block.scalar
        def _(a, hc=hc):
            ctx.wait(a, "tQ")
            for h in range(2):
                a.activation(QF[:, h * 512:(h + 1) * 512], P[h][:, 0:512],
                             AF.Identity, bias=QBS[:, hc:hc + 1])
            ctx.inc(a.activation(SQ[:], QF[:], AF.Square), "aE")
        qsq_ready = ctx.cnt["aE"]

        @block.tensor
        def _(t, qsq_ready=qsq_ready):
            t.wait_ge(ctx.sem["aE"], qsq_ready)
            t.matmul(P[4][0:2, 0:512], EQ1[:, 0:2], SQ[:, 0:512], start=True, stop=True)
            ctx.inc(t.matmul(P[5][0:2, 0:512], EQ1[:, 0:2], SQ[:, 512:1024],
                             start=True, stop=True), "tQ")
        ssq_done = ctx.cnt["tQ"]

        proj([P[2], P[3]], 1, wait_vE=M.get(("attnev",), 0))
        @block.scalar
        def _(a, hc=hc, ssq_done=ssq_done):
            ctx.wait(a, "tQ")
            for h in range(2):
                a.activation(KF[:, h * 512:(h + 1) * 512], P[2 + h][:, 0:512],
                             AF.Identity, bias=KBS[:, hc:hc + 1])
            a.wait_ge(ctx.sem["tQ"], ssq_done)
            ctx.inc(a.activation(SQ[:], KF[:], AF.Square), "aE")
        ksq_ready = ctx.cnt["aE"]

        @block.tensor
        def _(t, ksq_ready=ksq_ready):
            t.wait_ge(ctx.sem["aE"], ksq_ready)
            t.matmul(P[4][64:66, 0:512], EQ1[:, 0:2], SQ[:, 0:512], start=True, stop=True)
            ctx.inc(t.matmul(P[5][64:66, 0:512], EQ1[:, 0:2], SQ[:, 512:1024],
                             start=True, stop=True), "tQ")

        @block.scalar
        def _(a):
            ctx.wait(a, "tQ")
            a.activation(NRMQ[0:1, 0:512], P[4][0:1, 0:512], AF.Sqrt, scale=1.0 / 128.0)
            a.activation(NRMQ[0:1, 512:1024], P[5][0:1, 0:512], AF.Sqrt, scale=1.0 / 128.0)
            a.activation(NRMK[0:1, 0:512], P[4][64:65, 0:512], AF.Sqrt, scale=1.0 / 128.0)
            ctx.inc(a.activation(NRMK[0:1, 512:1024], P[5][64:65, 0:512], AF.Sqrt,
                                 scale=1.0 / 128.0), "aE")

        @block.vector
        def _(v):
            ctx.wait(v, "aE")
            v.tensor_scalar(NRMQ[:], NRMQ[:], 1e-12, None, ALU.max)
            v.tensor_scalar(NRMK[:], NRMK[:], 1e-12, None, ALU.max)
            v.reciprocal(NRMQ[:], NRMQ[:])
            ctx.inc(v.reciprocal(NRMK[:], NRMK[:]), "vE")
        inv_ready = ctx.cnt["vE"]

        @block.tensor
        def _(t, inv_ready=inv_ready):
            t.wait_ge(ctx.sem["vE"], inv_ready)
            t.matmul(P[4][:, 0:512], ONS[0:1, :], NRMQ[0:1, 0:512], start=True, stop=True)
            ctx.inc(t.matmul(P[5][:, 0:512], ONS[0:1, :], NRMQ[0:1, 512:1024],
                             start=True, stop=True), "tQ")

        @block.vector
        def _(v):
            ctx.wait(v, "tQ")
            v.tensor_mul(QB_[:, 0:512], QF[:, 0:512], P[4][:, 0:512])
            ctx.inc(v.tensor_mul(QB_[:, 512:1024], QF[:, 512:1024], P[5][:, 0:512]), "vE")
        qbf_ready = ctx.cnt["vE"]

        @block.tensor
        def _(t, qbf_ready=qbf_ready):
            t.wait_ge(ctx.sem["vE"], qbf_ready)
            t.matmul(P[4][:, 0:512], ONS[0:1, :], NRMK[0:1, 0:512], start=True, stop=True)
            ctx.inc(t.matmul(P[5][:, 0:512], ONS[0:1, :], NRMK[0:1, 512:1024],
                             start=True, stop=True), "tQ")

        @block.vector
        def _(v):
            ctx.wait(v, "tQ")
            v.tensor_mul(KB_[:, 0:512], KF[:, 0:512], P[4][:, 0:512])
            ctx.inc(v.tensor_mul(KB_[:, 512:1024], KF[:, 512:1024], P[5][:, 0:512]), "vE")
        kbf_ready = ctx.cnt["vE"]
        if stage == 'qk':
            continue

        proj([P[0], P[1]], 3, wait_aE=qsq_ready)
        M[("panelfree", slot)] = ctx.cnt["tQ"]
        @block.scalar
        def _(a):
            ctx.wait(a, "tQ")
            for h in range(2):
                act = a.activation(GT[:, h * 512:(h + 1) * 512], P[h][:, 0:512], AF.Silu)
            ctx.inc(act, "aE")
        gate_ready = ctx.cnt["aE"]
        M["p01_aE"] = gate_ready
        if stage == 'gate':
            continue

        for b in range(4):
            @block.tensor
            def _(t, b=b, kbf_ready=kbf_ready):
                t.wait_ge(ctx.sem["vE"], max(kbf_ready, M.get(("attnev",), 0)))
                for ch in range(4):
                    off = b * 256 + ch * 64
                    pb = (ch % 2) * 64
                    mm = t.matmul(PS7[pb:pb + 64, ch * 64:(ch + 1) * 64],
                                  KB_[:, off:off + 64], QB_[:, off:off + 64],
                                  start=True, stop=True)
                ctx.inc(mm, "tQ")

            @block.vector
            def _(v, b=b):
                ctx.wait(v, "tQ")
                for ch in range(4):
                    pb = (ch % 2) * 64
                    mu = v.tensor_mul(ST_[pb:pb + 64, ch * 64:(ch + 1) * 64],
                                      PS7[pb:pb + 64, ch * 64:(ch + 1) * 64],
                                      TRIU[pb:pb + 64, ch * 64:(ch + 1) * 64])
                ctx.inc(mu, "vE")
            st_ready = ctx.cnt["vE"]

            @block.tensor
            def _(t, b=b, hc=hc, st_ready=st_ready):
                t.wait_ge(ctx.sem["vE"], st_ready)
                for ch in range(4):
                    tcb = b * 2 + ch // 2
                    pb = (ch % 2) * 64
                    bank = P[2] if ch % 2 == 0 else P[3]
                    mm = t.matmul(bank[:, (ch // 2) * 64:(ch // 2) * 64 + 64],
                                  GV[pb:pb + 64, tcb * D + hc * 128: tcb * D + hc * 128 + 128],
                                  ST_[pb:pb + 64, ch * 64:(ch + 1) * 64],
                                  start=True, stop=True)
                ctx.inc(mm, "tQ")

            @block.vector
            def _(v, b=b, hc=hc, gate_ready=gate_ready):
                ctx.wait(v, "tQ")
                v.wait_ge(ctx.sem["aE"], gate_ready)
                for ch in range(4):
                    bank = P[2] if ch % 2 == 0 else P[3]
                    mu = v.tensor_mul(
                        OUTG[:, hc * TA + b * 256 + ch * 64: hc * TA + b * 256 + ch * 64 + 64],
                        bank[:, (ch // 2) * 64:(ch // 2) * 64 + 64],
                        GT[:, b * 256 + ch * 64: b * 256 + ch * 64 + 64])
                ctx.inc(mu, "vE")
            M[("attnev",)] = ctx.cnt["vE"]

    if stage in ('heads1', 'heads'):
        return
    last_gate = ctx.cnt["aE"]
    for os in range(4):
        @block.gpsimd
        def _(g, os=os):
            ctx.wait(g, "dWO")
            ctx.mwait(g, "tQ", ("wosfree",))
            for kc in range(16):
                ctx.inc(g.dma_start(
                    out=WOS[:, kc * 512:(kc + 1) * 512],
                    in_=io["woT"][kc * 128:(kc + 1) * 128, os * 512:(os + 1) * 512]),
                    "dWO", 16)
        wo_cnt = ctx.cnt["dWO"]

        for tc in range(NTC):
            it = os * NTC + tc
            pp = it % 2

            @block.sync
            def _(s, os=os, tc=tc, it=it):
                ctx.wait(s, "dX0")
                ctx.mwait(s, "vE", ("x0free",))
                ctx.inc(s.dma_start(
                    out=X0S[:],
                    in_=io["x0f"][t0 + tc * 128: t0 + tc * 128 + 128,
                                  os * 512:(os + 1) * 512]), "dX0", 16)
            x0_cnt = ctx.cnt["dX0"]

            @block.tensor
            def _(t, os=os, tc=tc, pp=pp, wo_cnt=wo_cnt, it=it):
                t.wait_ge(ctx.sem["dWO"], wo_cnt)
                if it < 2:
                    t.wait_ge(ctx.sem["aE"], last_gate)
                else:
                    ctx.mwait(t, "vE", ("oprojfree", pp))
                for kc in range(16):
                    mm = t.matmul(P[pp][:, 0:512],
                                  OUTG[:, kc * TA + tc * 128: kc * TA + tc * 128 + 128],
                                  WOS[:, kc * 512:(kc + 1) * 512],
                                  start=(kc == 0), stop=(kc == 15))
                ctx.inc(mm, "tQ")
            M[("wosfree",)] = ctx.cnt["tQ"]

            @block.vector
            def _(v, os=os, pp=pp, x0_cnt=x0_cnt):
                ctx.wait(v, "tQ")
                v.wait_ge(ctx.sem["dX0"], x0_cnt)
                ctx.wait(v, "dRS", ctx.marks.get(("rslfree",), 0))
                v.tensor_add(RSL[:], P[pp][:, 0:512], X0S[:])
                ctx.inc(v.tensor_add(RSL[:], RSL[:],
                                     OBB[:, os * 512:(os + 1) * 512]), "vE")
            M[("oprojfree", pp)] = ctx.cnt["vE"]
            M[("x0free",)] = ctx.cnt["vE"]
            r_evac = ctx.cnt["vE"]

            @block.scalar
            def _(a, os=os, tc=tc, r_evac=r_evac):
                ctx.wait(a, "dRS")
                a.wait_ge(ctx.sem["vE"], r_evac)
                ctx.inc(a.dma_start(
                    out=io["r1D"][t0 + tc * 128: t0 + tc * 128 + 128,
                                  os * 512:(os + 1) * 512],
                    in_=RSL[:]), "dRS", 16)
            M[("rslfree",)] = ctx.cnt["dRS"]


def _emit_lnf(ctx, sb, rD, xlnf, xlnb, rows, dbg=None, xlnb2=None):
    block = ctx.block
    R1, XLB, STT, EPS = sb["R1"], sb["XLB"], sb["STT"], sb["EPS"]
    if isinstance(R1, list):
        R1 = R1[0]
    for i, r0 in enumerate(rows):
        @block.sync
        def _(s, r0=r0):
            ctx.wait(s, "dLR")
            ctx.wait(s, "dRS")
            ctx.wait(s, "dLS", ctx.marks.get(("r1free",), 0))
            ctx.inc(s.dma_start(out=R1[:], in_=rD[r0:r0 + 128, :]), "dLR", 16)
        rload = ctx.cnt["dLR"]

        @block.vector
        def _(v, rload=rload):
            v.wait_ge(ctx.sem["dLR"], rload)
            ctx.wait(v, "dLS", ctx.marks.get(("xlbfree",), 0))
            ctx.inc(v.reduce_sum(STT[:, 0:1], R1[:], AX), "vE")

        @block.scalar
        def _(a):
            ctx.wait(a, "vE")
            ctx.inc(a.activation(XLB[:], R1[:], AF.Square), "aE")

        @block.vector
        def _(v):
            ctx.wait(v, "aE")
            v.reduce_sum(STT[:, 1:2], XLB[:], AX)
            ctx.inc(v.tensor_scalar(STT[:, 2:3], STT[:, 0:1], 1.0 / D, None, ALU.mult), "vE")
            ctx.wait(v, "vE")
            v.tensor_mul(STT[:, 3:4], STT[:, 2:3], STT[:, 2:3])
            ctx.inc(v.tensor_scalar(STT[:, 4:5], STT[:, 1:2], 1.0 / D, None, ALU.mult), "vE")
            ctx.wait(v, "vE")
            ctx.inc(v.tensor_sub(STT[:, 4:5], STT[:, 4:5], STT[:, 3:4]), "vE")

        @block.scalar
        def _(a):
            ctx.wait(a, "vE")
            ctx.inc(a.activation(STT[:, 5:6], STT[:, 4:5], AF.Sqrt,
                                 bias=EPS[:, 0:1]), "aE")

        @block.vector
        def _(v):
            ctx.wait(v, "aE")
            ctx.inc(v.reciprocal(STT[:, 6:7], STT[:, 5:6]), "vE")
            ctx.wait(v, "vE")
            ctx.inc(v.tensor_scalar(XLB[:], R1[:], STT[:, 2:3], STT[:, 6:7],
                                    ALU.subtract, ALU.mult), "vE")
            ctx.wait(v, "vE")
            ctx.inc(v.tensor_scalar(R1[:], R1[:], STT[:, 2:3], STT[:, 6:7],
                                    ALU.subtract, ALU.mult), "vE")
        xln_ready = ctx.cnt["vE"]

        @block.scalar
        def _(a, r0=r0, xln_ready=xln_ready):
            ctx.wait(a, "dLS")
            a.wait_ge(ctx.sem["vE"], xln_ready)
            ctx.inc(a.dma_start(out=xlnf[r0:r0 + 128, :], in_=R1[:]), "dLS", 16)
            if xlnb is not None:
                ctx.inc(a.dma_start(out=xlnb[r0:r0 + 128, :], in_=XLB[:]), "dLS", 16)
            if xlnb2 is not None:
                b_ = r0 // 256
                lst = r0 % 256
                dst = xlnb2.rearrange("(l b) d -> l b d", b=B)[lst:lst + 128, b_, :]
                ctx.inc(a.dma_start(out=dst, in_=XLB[:]), "dLS", 16)
            if dbg is not None:
                ctx.inc(a.dma_start(out=dbg[r0:r0 + 128, 0:7], in_=STT[:, 0:7]), "dLS", 16)
        ctx.marks[("r1free",)] = ctx.cnt["dLS"]
        ctx.marks[("xlbfree",)] = ctx.cnt["dLS"]


def _cp_layout(wT):
    K, Mo = wT.shape
    a = wT.reshape(K // 128, 128, Mo // 128, 128)
    return np.ascontiguousarray(a.transpose(2, 1, 0, 3).reshape(Mo, K))


def _prep_a(inp):
    d = {}
    d["wq_cp"] = _cp_layout(np.ascontiguousarray(inp["L0_attn_q_w"].T))
    d["wk_cp"] = _cp_layout(np.ascontiguousarray(inp["L0_attn_k_w"].T))
    d["wv_cp"] = _cp_layout(np.ascontiguousarray(inp["L0_attn_v_w"].T))
    d["wog_cp"] = _cp_layout(np.ascontiguousarray(inp["L0_attn_og_w"].T))
    d["woT"] = np.ascontiguousarray(inp["L0_attn_o_w"].T)
    d["qb_r"] = np.ascontiguousarray(inp["L0_attn_q_b"].reshape(16, 128).T)
    d["kb_r"] = np.ascontiguousarray(inp["L0_attn_k_b"].reshape(16, 128).T)
    d["vb_r"] = np.ascontiguousarray(inp["L0_attn_v_b"].reshape(16, 128).T)
    d["obrow"] = np.ascontiguousarray(inp["L0_attn_o_b"].reshape(1, D))
    bw = inp["L0_attn_b_w"]
    d["bwT_r"] = np.ascontiguousarray(
        bw.T.reshape(16, 128, 16).transpose(1, 0, 2).reshape(128, 256))
    d["bb_r"] = np.ascontiguousarray(inp["L0_attn_b_b"].reshape(1, 16))
    return d


def _consts():
    c = {}
    c["c_ones_s"] = np.full((1, 128), 1.0 / np.sqrt(128.0), np.float32)
    c["c_ones"] = np.ones((1, 128), np.float32)
    triu = np.triu(np.ones((64, 64), np.float32))
    c["c_triu4"] = np.ascontiguousarray(np.tile(triu, (2, 4)))
    c["c_ident"] = np.eye(128).astype(BF16NP)
    c["c_identf"] = np.eye(128, dtype=np.float32)
    c["c_epsw"] = np.tile((np.arange(7, -1, -1, dtype=np.float32) * (2.0 ** -18))[None, :],
                          (128, 1))
    return c


def _slice_core(x0, c):
    return np.ascontiguousarray(x0[:, c * LC:(c + 1) * LC, :]).reshape(NT, D)


def kernel(**inputs):
    inp = {k: np.asarray(v) for k, v in inputs.items()}
    consts = _consts()
    pa, pb, pc, pd = _prep_a(inp), _prep_b(inp), _prep_c(inp), _prep_d(inp)
    shared = {**consts, **pa, **pb, **pc, **pd}
    x0 = inp["L0_attn_hidden_states"]
    cores = list(range(NCORES))

    mids = [dict() for _ in cores]
    zeros_x0 = None
    for phases, outs, needs in [
        (("A",), ["xln1f", "xln1b"], []),
        (("B",), ["xln2f", "xln2c"], ["xln1f", "xln1b"]),
        (("C",), ["xln3f", "xln3b"], ["xln2f", "xln2c"]),
        (("D",), ["xout"], ["xln3f", "xln3b"]),
    ]:
        nc, io = build_program(phases=phases, external_mid=True)
        names = _input_names(nc)
        in_maps = []
        for c in cores:
            m = {}
            for n in names:
                if n == "x0f":
                    m[n] = _slice_core(x0, c)
                elif n in mids[c]:
                    m[n] = mids[c][n]
                elif n in shared:
                    m[n] = shared[n]
                else:
                    raise KeyError(f"missing input {n} for phases {phases}")
            in_maps.append(m)
        res = run_bass_kernel_spmd(nc, in_maps, core_ids=cores)
        for c in cores:
            for o in outs:
                mids[c][o] = np.asarray(res.results[c][o])

    out = np.empty((B, 2048, D), np.float32)
    for c in cores:
        out[:, c * LC:(c + 1) * LC, :] = mids[c]["xout"].reshape(B, LC, D)
    return out


def _input_names(nc):
    names = []
    import concourse.mybir as _mb
    for alloc in nc.m.functions[0].allocations:
        if not isinstance(alloc, _mb.MemoryLocationSet):
            continue
        if alloc.kind == "ExternalInput":
            n = alloc.memorylocations[0].name
            if "partition_id" not in n and "dbg_addr" not in n:
                names.append(n)
    return names


def emit_phase_b(ctx, io, ntiles=4):
    nc, block = ctx.nc, ctx.block
    TA = 1024
    NTC = 8

    al = nc.alloc_sbuf_tensor
    XFB = al("b_xf", [128, 16 * TA], BF16)
    XTB = al("b_xt", [128, D], F32)
    SWLU = al("b_swlu", [128, 16 * 128], BF16)
    MUB = al("b_mub", [128, 16 * 8], BF16)
    DEBB = al("b_debb", [128, 16 * 8], BF16)
    LDG = al("b_ldg", [64, D], BF16)
    BW8 = al("b_bw8", [8, D], BF16)
    CM = al("b_cm", [64, 8], F32)
    I8 = al("b_i8", [8, 8], F32)
    BO8 = al("b_bo8", [64, 8], F32)
    E8 = al("b_e8", [8, 64], BF16)
    CN = al("b_cn", [64, 1], F32)
    U0 = al("b_u0", [64, 1], F32)
    MU2 = al("b_mu2", [8, 1], F32)
    CEN = al("b_cen", [8, 1], F32)
    SLP = al("b_slp", [8, 1], F32)
    SQB = al("b_sqb", [128, TA], F32)
    XSR = al("b_xsr", [1, TA], F32)
    PP = al("b_pp", [128, TA], F32)
    PC2 = al("b_pc2", [64, TA], F32)
    MX8 = al("b_mx8", [8, TA], F32)
    NR8 = al("b_nr8", [8, TA], F32)
    GT8 = al("b_gt8", [8, TA], F32)
    GTT = al("b_gtt", [128, NTC * 8], F32)
    WTT = al("b_wtt", [128, NTC * 8], F32)
    MSEL = al("b_msel", [128, 2 * NTC], F32)
    W8 = al("b_w8", [8, TA], BF16)
    WM = al("b_wm", [64, TA], BF16)
    BB5 = al("b_bb5", [128, 512], F32)
    R4S = al("b_r4s", [128, 512], F32)
    ONE8 = al("b_one8", [1, 8], F32)
    EB2 = al("b_eb2", [128, 2], F32)
    IDNF = al("b_idnf", [128, 128], F32)
    EPSW = al("b_epsw", [128, 8], F32)

    P = [nc.alloc_psum_tensor(f"b_p{i}", [128, 512], F32) for i in range(8)]

    sbB = dict(R1=al("b_r1", [128, D], F32), XLB=al("b_xlb", [128, D], BF16),
               STT=al("b_stt", [128, 8], F32), EPS=al("b_eps", [128, 1], F32))

    @block.vector
    def _(v):
        v.memset(sbB["EPS"][:], 1e-5)
        v.memset(EB2[:], 1.0)
        ctx.inc(v.memset(ONE8[:], 1.0), "vE")

    @block.sync
    def _(s):
        ctx.wait(s, "dC")
        for dst, srcn in [(CM, "c_cmask"), (I8, "c_i8"), (BO8, "c_bo8"),
                          (CEN, "cen8"), (SLP, "slp8"), (IDNF, "c_identf"),
                          (EPSW, "c_epsw")]:
            ctx.inc(s.dma_start(out=dst[:], in_=io[srcn]), "dC", 16)

    @block.gpsimd
    def _(g):
        ctx.wait(g, "dXB")
        for kc in range(16):
            ctx.inc(g.dma_start(out=SWLU[:, kc * 128:kc * 128 + 64],
                                in_=io["sw_r"][kc * 128:(kc + 1) * 128, :]), "dXB", 16)
            ctx.inc(g.dma_start(out=SWLU[:, kc * 128 + 64:kc * 128 + 128],
                                in_=io["lu_r"][kc * 128:(kc + 1) * 128, :]), "dXB", 16)
            ctx.inc(g.dma_start(out=MUB[:, kc * 8:(kc + 1) * 8],
                                in_=io["mu_r"][kc * 128:(kc + 1) * 128, :]), "dXB", 16)
            ctx.inc(g.dma_start(out=DEBB[:, kc * 8:(kc + 1) * 8],
                                in_=io["deb_r"][kc * 128:(kc + 1) * 128, :]), "dXB", 16)
        ctx.inc(g.dma_start(out=LDG[:], in_=io["ldg_r"]), "dXB", 16)
        ctx.inc(g.dma_start(out=E8[:], in_=io["c_e8"]), "dXB", 16)
        ctx.inc(g.dma_start(out=BW8[:], in_=io["bw8"]), "dXB", 16)
    wload = ctx.cnt["dXB"]

    @block.tensor
    def _(t):
        t.wait_ge(ctx.sem["dXB"], wload)
        ctx.wait(t, "dC")
        ctx.wait(t, "vE")
        ctx.wait(t, "tQ")
        for kc in range(16):
            t.matmul(P[0][0:64, 0:8], SWLU[:, kc * 128:kc * 128 + 64],
                     MUB[:, kc * 8:(kc + 1) * 8], start=(kc == 0), stop=(kc == 15))
        for kc in range(16):
            t.matmul(P[1][0:64, 0:8], SWLU[:, kc * 128 + 64:kc * 128 + 128],
                     DEBB[:, kc * 8:(kc + 1) * 8], start=(kc == 0), stop=(kc == 15))
        for kc in range(16):
            mm = t.matmul(P[2][0:8, 0:8], MUB[:, kc * 8:(kc + 1) * 8],
                          MUB[:, kc * 8:(kc + 1) * 8], start=(kc == 0), stop=(kc == 15))
        ctx.inc(mm, "tQ")

    @block.vector
    def _(v):
        ctx.wait(v, "tQ")
        v.tensor_mul(PC2[:, 0:8], P[0][0:64, 0:8], CM[:])
        v.tensor_mul(PC2[:, 8:16], P[1][0:64, 0:8], CM[:])
        ctx.inc(v.tensor_mul(NR8[:, 0:8], P[2][0:8, 0:8], I8[:]), "vE")
        ctx.wait(v, "vE")
        v.reduce_sum(CN[:], PC2[:, 0:8], AX)
        v.reduce_sum(U0[:], PC2[:, 8:16], AX)
        ctx.inc(v.reduce_sum(MU2[:], NR8[:, 0:8], AX), "vE")
    pre_done = ctx.cnt["vE"]

    for ti in range(ntiles):
        _emit_b_tile(ctx, io, ti, TA, NTC, XFB, XTB, SWLU, MUB, LDG, BW8, BO8,
                     E8, CN, U0, MU2, CEN, SLP, SQB, XSR, PP, PC2, MX8, NR8,
                     GT8, GTT, WTT, MSEL, W8, WM, BB5, R4S, ONE8, EB2, IDNF,
                     EPSW, P, pre_done)
        _emit_lnf(ctx, sbB, io["r2D"], io["xln2f"], io["xln2b"],
                  [ti * TA + k * 128 for k in range(NTC)], xlnb2=io["xln2c"])


def _emit_b_tile(ctx, io, ti, TA, NTC, XFB, XTB, SWLU, MUB, LDG, BW8, BO8,
                 E8, CN, U0, MU2, CEN, SLP, SQB, XSR, PP, PC2, MX8, NR8,
                 GT8, GTT, WTT, MSEL, W8, WM, BB5, R4S, ONE8, EB2, IDNF,
                 EPSW, P, pre_done):
    nc, block = ctx.nc, ctx.block
    t0 = ti * TA
    TAH = 512

    @block.sync
    def _(s):
        ctx.wait(s, "dXF")
        if ti > 0:
            ctx.wait(s, "tQ")
            ctx.wait(s, "vE")
        for c in range(16):
            ctx.inc(s.dma_start_transpose(
                XFB[:, c * TA:(c + 1) * TA],
                io["xln1b"][t0:t0 + TA, c * 128:(c + 1) * 128]), "dXF", 16)
    xf_cnt = ctx.cnt["dXF"]

    for kc in range(16):
        @block.scalar
        def _(a, kc=kc):
            if kc == 0:
                a.wait_ge(ctx.sem["dXF"], xf_cnt)
                ctx.wait(a, "vE")
                a.wait_ge(ctx.sem["vE"], pre_done)
            else:
                ctx.wait(a, "tQ")
            ctx.inc(a.activation(SQB[:], XFB[:, kc * TA:(kc + 1) * TA],
                                 AF.Square), "aE")
        @block.tensor
        def _(t, kc=kc):
            ctx.wait(t, "aE")
            if kc == 0:
                ctx.wait(t, "vE")
            t.matmul(P[6][0:2, 0:512], EB2[:, 0:2], SQB[:, 0:512],
                     start=(kc == 0), stop=(kc == 15))
            ctx.inc(t.matmul(P[7][0:2, 0:512], EB2[:, 0:2], SQB[:, 512:1024],
                             start=(kc == 0), stop=(kc == 15)), "tQ")
    @block.vector
    def _(v):
        ctx.wait(v, "tQ")
        v.tensor_copy(XSR[0:1, 0:512], P[6][0:1, 0:512])
        ctx.inc(v.tensor_copy(XSR[0:1, 512:1024], P[7][0:1, 0:512]), "vE")

    for h in range(2):
        @block.tensor
        def _(t, h=h):
            ctx.wait(t, "vE")
            for kc in range(16):
                mm = t.matmul(P[h][:, 0:512], SWLU[:, kc * 128:(kc + 1) * 128],
                              XFB[:, kc * TA + h * 512: kc * TA + h * 512 + 512],
                              start=(kc == 0), stop=(kc == 15))
            ctx.inc(mm, "tQ")
    @block.vector
    def _(v):
        ctx.wait(v, "tQ")
        v.tensor_copy(PP[:, 0:512], P[0][:, 0:512])
        ctx.inc(v.tensor_copy(PP[:, 512:1024], P[1][:, 0:512]), "vE")
    pp_done = ctx.cnt["vE"]

    for h in range(2):
        @block.tensor
        def _(t, h=h):
            for kc in range(16):
                mm = t.matmul(P[2 + h][0:8, 0:512], MUB[:, kc * 8:(kc + 1) * 8],
                              XFB[:, kc * TA + h * 512: kc * TA + h * 512 + 512],
                              start=(kc == 0), stop=(kc == 15))
            ctx.inc(mm, "tQ")
    @block.vector
    def _(v):
        ctx.wait(v, "tQ")
        v.tensor_copy(MX8[:, 0:512], P[2][0:8, 0:512])
        ctx.inc(v.tensor_copy(MX8[:, 512:1024], P[3][0:8, 0:512]), "vE")

    @block.vector
    def _(v):
        ctx.wait(v, "vE")
        ctx.inc(v.tensor_scalar(PC2[:], PP[0:64, :], CN[:, 0:1], None,
                                ALU.subtract), "vE")
    @block.scalar
    def _(a):
        ctx.wait(a, "vE")
        ctx.inc(a.activation(PC2[:], PC2[:], AF.Square), "aE")
    @block.tensor
    def _(t):
        ctx.wait(t, "aE")
        t.matmul(P[4][0:8, 0:512], BO8[:], PC2[:, 0:512], start=True, stop=True)
        ctx.inc(t.matmul(P[5][0:8, 0:512], BO8[:], PC2[:, 512:1024],
                         start=True, stop=True), "tQ")

    @block.tensor
    def _(t):
        t.matmul(P[6][0:8, 0:512], ONE8[0:1, :], XSR[0:1, 0:512], start=True, stop=True)
        ctx.inc(t.matmul(P[7][0:8, 0:512], ONE8[0:1, :], XSR[0:1, 512:1024],
                         start=True, stop=True), "tQ")

    @block.vector
    def _(v):
        ctx.wait(v, "tQ")
        v.scalar_tensor_tensor(NR8[:, 0:512], MX8[:, 0:512], -2.0,
                               P[6][0:8, 0:512], ALU.mult, ALU.add)
        ctx.inc(v.scalar_tensor_tensor(NR8[:, 512:1024], MX8[:, 512:1024], -2.0,
                                       P[7][0:8, 0:512], ALU.mult, ALU.add), "vE")
        ctx.wait(v, "vE")
        ctx.inc(v.tensor_scalar(NR8[:], NR8[:], MU2[:, 0:1], None, ALU.add), "vE")
        ctx.wait(v, "vE")
        ctx.inc(v.reciprocal(NR8[:], NR8[:]), "vE")
        ctx.wait(v, "vE")
        v.tensor_mul(GT8[:, 0:512], P[4][0:8, 0:512], NR8[:, 0:512])
        ctx.inc(v.tensor_mul(GT8[:, 512:1024], P[5][0:8, 0:512],
                             NR8[:, 512:1024]), "vE")
        ctx.wait(v, "vE")
        ctx.inc(v.tensor_scalar(GT8[:], GT8[:], CEN[:, 0:1], SLP[:, 0:1],
                                ALU.subtract, ALU.mult), "vE")
    @block.scalar
    def _(a):
        ctx.wait(a, "vE")
        ctx.inc(a.activation(GT8[:], GT8[:], AF.Sigmoid), "aE")
    gate_done = ctx.cnt["aE"]

    for tc in range(NTC):
        pb = tc % 2
        @block.tensor
        def _(t, tc=tc, pb=pb, gate_done=gate_done):
            if tc == 0:
                t.wait_ge(ctx.sem["aE"], gate_done)
            ctx.wait(t, "vE")
            ctx.inc(t.transpose(P[6 + pb][:, 0:8], GT8[:, tc * 128:(tc + 1) * 128],
                                IDNF[0:8, 0:8]), "tQ")
        @block.vector
        def _(v, tc=tc, pb=pb):
            ctx.wait(v, "tQ")
            ctx.inc(v.tensor_copy(GTT[:, tc * 8:(tc + 1) * 8], P[6 + pb][:, 0:8]), "vE")
    for tc in range(NTC):
        @block.vector
        def _(v, tc=tc):
            sl = slice(tc * 8, (tc + 1) * 8)
            ctx.wait(v, "vE")
            ctx.inc(v.tensor_add(WTT[:, sl], GTT[:, sl], EPSW[:]), "vE")
            ctx.wait(v, "vE")
            ctx.inc(v.reduce_max(MSEL[:, tc + 8:tc + 9], WTT[:, sl], AX), "vE")
            ctx.wait(v, "vE")
            ctx.inc(v.tensor_scalar(WTT[:, sl], WTT[:, sl],
                                    MSEL[:, tc + 8:tc + 9], None, ALU.is_ge), "vE")
            ctx.inc(v.reduce_max(MSEL[:, tc:tc + 1], GTT[:, sl], AX), "vE")
    for tc in range(NTC):
        pb = tc % 2
        @block.tensor
        def _(t, tc=tc, pb=pb):
            ctx.wait(t, "vE")
            ctx.inc(t.transpose(P[6 + pb][0:8, 0:128],
                                WTT[:, tc * 8:(tc + 1) * 8], IDNF[:]), "tQ")
        @block.vector
        def _(v, tc=tc, pb=pb):
            ctx.wait(v, "tQ")
            ctx.inc(v.tensor_copy(W8[0:8, tc * 128:(tc + 1) * 128],
                                  P[6 + pb][0:8, 0:128]), "vE")
    w8_done = ctx.cnt["vE"]

    @block.vector
    def _(v, pp_done=pp_done):
        v.wait_ge(ctx.sem["vE"], pp_done)
        ctx.inc(v.tensor_scalar(PC2[:], PP[64:128, :], U0[:, 0:1], None,
                                ALU.subtract), "vE")
    pm_done = ctx.cnt["vE"]
    @block.tensor
    def _(t, w8_done=w8_done):
        t.wait_ge(ctx.sem["vE"], max(w8_done, pm_done))
        t.matmul(P[0][0:64, 0:512], E8[:], W8[0:8, 0:512], start=True, stop=True)
        ctx.inc(t.matmul(P[1][0:64, 0:512], E8[:], W8[0:8, 512:1024],
                         start=True, stop=True), "tQ")
    @block.vector
    def _(v):
        ctx.wait(v, "tQ")
        v.tensor_mul(WM[:, 0:512], PC2[:, 0:512], P[0][0:64, 0:512])
        ctx.inc(v.tensor_mul(WM[:, 512:1024], PC2[:, 512:1024],
                             P[1][0:64, 0:512]), "vE")
    wm_done = ctx.cnt["vE"]

    for tc in range(NTC):
        @block.sync
        def _(s, tc=tc):
            ctx.wait(s, "dX0")
            ctx.mwait(s, "vE", ("bxtfree",))
            ctx.inc(s.dma_start(out=XTB[:],
                                in_=io["xln1f"][t0 + tc * 128: t0 + tc * 128 + 128, :]),
                    "dX0", 16)
        xt_cnt = ctx.cnt["dX0"]
        for ds in range(4):
            pd = ds % 2
            @block.tensor
            def _(t, tc=tc, ds=ds, pd=pd, wm_done=wm_done):
                if tc == 0 and ds == 0:
                    t.wait_ge(ctx.sem["vE"], wm_done)
                ctx.mwait(t, "vE", ("bdfree", pd))
                t.matmul(P[2 + pd][:, 0:512], WM[:, tc * 128:(tc + 1) * 128],
                         LDG[:, ds * 512:(ds + 1) * 512], start=True, stop=True)
                ctx.inc(t.matmul(P[4 + pd][:, 0:512], W8[0:8, tc * 128:(tc + 1) * 128],
                                 BW8[:, ds * 512:(ds + 1) * 512],
                                 start=True, stop=True), "tQ")
            @block.vector
            def _(v, tc=tc, ds=ds, pd=pd, xt_cnt=xt_cnt):
                ctx.wait(v, "tQ")
                v.wait_ge(ctx.sem["dX0"], xt_cnt)
                ctx.wait(v, "dRS", ctx.marks.get(("brslfree",), 0))
                sl = slice(ds * 512, (ds + 1) * 512)
                ctx.inc(v.tensor_copy(BB5[:], P[4 + pd][:, 0:512]), "vE")
                ctx.wait(v, "vE")
                v.scalar_tensor_tensor(R4S[:], P[2 + pd][:, 0:512], 1.0, BB5[:],
                                       ALU.mult, ALU.add)
                ctx.inc(v.tensor_sub(R4S[:], R4S[:], XTB[:, sl]), "vE")
                ctx.wait(v, "vE")
                v.tensor_scalar(R4S[:], R4S[:], MSEL[:, tc:tc + 1], None, ALU.mult)
                ctx.inc(v.scalar_tensor_tensor(R4S[:], XTB[:, sl], 2.0, R4S[:],
                                               ALU.mult, ALU.add), "vE")
            ctx.marks[("bdfree", pd)] = ctx.cnt["vE"]
            r_ev = ctx.cnt["vE"]
            @block.scalar
            def _(a, tc=tc, ds=ds, r_ev=r_ev):
                ctx.wait(a, "dRS")
                a.wait_ge(ctx.sem["vE"], r_ev)
                ctx.inc(a.dma_start(
                    out=io["r2D"][t0 + tc * 128: t0 + tc * 128 + 128,
                                  ds * 512:(ds + 1) * 512], in_=R4S[:]), "dRS", 16)
            ctx.marks[("brslfree",)] = ctx.cnt["dRS"]
        ctx.marks[("bxtfree",)] = ctx.cnt["vE"]


def _prep_b(inp):
    d = {}
    d["sw_r"] = np.ascontiguousarray(
        inp["L0_ffn_select_weight"].transpose(1, 0, 2).reshape(D, 64))
    d["lu_r"] = np.ascontiguousarray(
        inp["L0_ffn_lora_update"].transpose(1, 0, 2).reshape(D, 64))
    d["mu_r"] = np.ascontiguousarray(inp["L0_ffn_select_mean_diff"].T)
    d["deb_r"] = np.ascontiguousarray(inp["L0_ffn_debias_w"].T)
    d["ldg_r"] = np.ascontiguousarray(
        inp["L0_ffn_lora_degen"].transpose(0, 2, 1).reshape(64, D))
    d["bw8"] = np.ascontiguousarray(inp["L0_ffn_bias_w"])
    d["cen8"] = np.ascontiguousarray(inp["L0_ffn_imp_center"].reshape(8, 1))
    d["slp8"] = np.ascontiguousarray(inp["L0_ffn_imp_slope"].reshape(8, 1))
    cm = np.zeros((64, 8), np.float32)
    for n in range(8):
        cm[n * 8:(n + 1) * 8, n] = 1.0
    d["c_cmask"] = cm
    d["c_i8"] = np.eye(8, dtype=np.float32)
    d["c_bo8"] = np.ascontiguousarray(cm)
    e8 = np.zeros((8, 64), np.float32)
    for n in range(8):
        e8[n, n * 8:(n + 1) * 8] = 1.0
    d["c_e8"] = e8.astype(BF16NP)
    return d


def emit_phase_c(ctx, io):
    nc, block = ctx.nc, ctx.block
    TA = 1024
    NTC = 8

    al = nc.alloc_sbuf_tensor
    XFC = al("c_xf", [128, 16 * TA], BF16)
    QHC = al("c_qh", [128, 16 * TA], BF16)
    K2 = al("c_k2", [128, 2 * TA], BF16)
    V2 = al("c_v2", [128, 2 * TA], BF16)
    QOUT = al("c_qout", [128, 16 * TA], BF16)
    WPC = [al(f"c_wp{i}", [128, D], BF16) for i in range(2)]
    WOSC = al("c_wos", [128, 16 * 512], BF16)
    EH = al("c_eh", [128, 512], BF16)
    SE = al("c_se", [128, 128], F32)
    MXC = al("c_mx", [128, 8], F32)
    RSC = al("c_rs", [128, 8], F32)
    RINV = al("c_rinv", [128, 32], F32)
    ETS = al("c_ets", [16, 512], BF16)
    VTS = al("c_vts", [16, 256], BF16)
    AOS = al("c_aos", [128, 256], BF16)
    XCS = al("c_xcs", [128, 512], F32)
    RCS = al("c_rcs", [128, 512], F32)
    OPBC = al("c_opb", [128, D], F32)
    QBSC = al("c_qbs", [128, 16], F32)
    KVBS = al("c_kvb", [128, 4], F32)
    ONEC = al("c_one", [1, 128], F32)
    OBRC = al("c_obr", [1, 512], F32)
    IDNC = al("c_idn", [128, 128], BF16)

    P = [nc.alloc_psum_tensor(f"c_p{i}", [128, 512], F32) for i in range(3)]
    PETB = nc.alloc_psum_tensor("c_pet", [16, 512], BF16)
    PVT = [nc.alloc_psum_tensor(f"c_pvt{i}", [16, 256], BF16) for i in range(2)]
    PAP = nc.alloc_psum_tensor("c_pap", [128, 256], F32)
    POT = nc.alloc_psum_tensor("c_pot", [64, 512], BF16)

    sbC = dict(R1=al("c_r1", [128, D], F32), XLB=al("c_xlb", [128, D], BF16),
               STT=al("c_stt", [128, 8], F32), EPS=al("c_eps", [128, 1], F32))

    @block.vector
    def _(v):
        ctx.inc(v.memset(sbC["EPS"][:], 1e-5), "vE")
        ctx.inc(v.memset(ONEC[:], 1.0), "vE")

    @block.sync
    def _(s):
        ctx.wait(s, "dC")
        ctx.inc(s.dma_start(out=QBSC[:], in_=io["qbc_r"]), "dC", 16)
        ctx.inc(s.dma_start(out=KVBS[:], in_=io["kvbc_r"]), "dC", 16)
        ctx.inc(s.dma_start(out=IDNC[:], in_=io["c_ident"]), "dC", 16)

    for os in range(4):
        @block.sync
        def _(s, os=os):
            ctx.wait(s, "dC")
            if os > 0:
                ctx.wait(s, "tQ")
            ctx.inc(s.dma_start(out=OBRC[:],
                                in_=io["obc_row"][0:1, os * 512:(os + 1) * 512]), "dC", 16)
        @block.tensor
        def _(t, os=os):
            ctx.wait(t, "dC")
            ctx.wait(t, "vE")
            if os == 0:
                ctx.wait(t, "tQ")
            ctx.inc(t.matmul(P[2][:, 0:512], ONEC[0:1, :], OBRC[0:1, 0:512],
                             start=True, stop=True), "tQ")
        @block.vector
        def _(v, os=os):
            ctx.wait(v, "tQ")
            ctx.inc(v.tensor_copy(OPBC[:, os * 512:(os + 1) * 512], P[2][:, 0:512]), "vE")

    xv = io["xln2c"]
    xvf = io["xln2f"].rearrange("(b l) d -> l b d", b=B)
    r3v = io["r3D"].rearrange("(b l) d -> l b d", b=B)
    for ti in range(4):
        _emit_c_tile(ctx, io, ti, xv, xvf, XFC, QHC, K2, V2, QOUT, WPC, WOSC,
                     EH, SE, MXC, RSC, RINV, ETS, VTS, AOS, XCS, RCS, OPBC,
                     QBSC, KVBS, IDNC, P, PETB, PVT, PAP, POT, r3v=r3v)
        if ti % 2 == 1:
            rows = []
            for k in range(32):
                if (k % 2) == (ti // 2):
                    rows.append(k * 128)
            _emit_lnf(ctx, sbC, io["r3D"], io["xln3f"], io["xln3b"], rows)


def _emit_c_tile(ctx, io, ti, xv, xvf, XFC, QHC, K2, V2, QOUT, WPC, WOSC,
                 EH, SE, MXC, RSC, RINV, ETS, VTS, AOS, XCS, RCS, OPBC,
                 QBSC, KVBS, IDNC, P, PETB, PVT, PAP, POT, r3v=None):
    nc, block = ctx.nc, ctx.block
    TA = 1024
    l0 = ti * 64
    M = ctx.marks

    @block.sync
    def _(s):
        ctx.wait(s, "dXF")
        if ti > 0:
            ctx.wait(s, "tQ")
            ctx.wait(s, "vE")
        for c in range(16):
            src = xv[l0 * 16:(l0 + 64) * 16, c * 128:(c + 1) * 128]
            ctx.inc(s.dma_start_transpose(XFC[:, c * TA:(c + 1) * TA], src), "dXF", 16)
    xf_cnt = ctx.cnt["dXF"]

    for oc in range(16):
        slot = oc % 2
        psem = "dP0" if slot == 0 else "dP1"
        @block.gpsimd
        def _(g, oc=oc, slot=slot, psem=psem):
            ctx.wait(g, psem)
            ctx.mwait(g, "tQ", ("cpanelfree", slot))
            ctx.inc(g.dma_start(out=WPC[slot][:],
                                in_=io["wqc_cp"][oc * 128:(oc + 1) * 128, :]), psem, 16)
        pcnt = ctx.cnt[psem]
        @block.tensor
        def _(t, oc=oc, slot=slot, psem=psem, pcnt=pcnt):
            t.wait_ge(ctx.sem[psem], pcnt)
            if oc == 0:
                t.wait_ge(ctx.sem["dXF"], xf_cnt)
            ctx.mwait(t, "aE", ("cqevac",))
            for h in range(2):
                for kc in range(16):
                    mm = t.matmul(P[h][:, 0:512],
                                  WPC[slot][:, kc * 128:(kc + 1) * 128],
                                  XFC[:, kc * TA + h * 512: kc * TA + h * 512 + 512],
                                  start=(kc == 0), stop=(kc == 15))
            ctx.inc(mm, "tQ")
        M[("cpanelfree", slot)] = ctx.cnt["tQ"]
        @block.scalar
        def _(a, oc=oc):
            ctx.wait(a, "tQ")
            g_, c_ = oc // 2, oc % 2
            for h in range(2):
                dst = _ap3(QHC, 0, 128, c_ * 8192 + h * 4096 + g_ * 16,
                           [(32, 128), (16, 1)])
                act = a.activation(dst, P[h][:, 0:512], AF.Identity,
                                   bias=QBSC[:, oc:oc + 1])
            ctx.inc(act, "aE")
        M[("cqevac",)] = ctx.cnt["aE"]

    for oc in range(4):
        slot = oc % 2
        psem = "dP0" if slot == 0 else "dP1"
        @block.gpsimd
        def _(g, oc=oc, slot=slot, psem=psem):
            ctx.wait(g, psem)
            ctx.mwait(g, "tQ", ("cpanelfree", slot))
            ctx.inc(g.dma_start(out=WPC[slot][:],
                                in_=io["wkvc_cp"][oc * 128:(oc + 1) * 128, :]), psem, 16)
        pcnt = ctx.cnt[psem]
        @block.tensor
        def _(t, oc=oc, slot=slot, psem=psem, pcnt=pcnt):
            t.wait_ge(ctx.sem[psem], pcnt)
            ctx.mwait(t, "aE", ("cqevac",))
            for h in range(2):
                for kc in range(16):
                    mm = t.matmul(P[h][:, 0:512],
                                  WPC[slot][:, kc * 128:(kc + 1) * 128],
                                  XFC[:, kc * TA + h * 512: kc * TA + h * 512 + 512],
                                  start=(kc == 0), stop=(kc == 15))
            ctx.inc(mm, "tQ")
        M[("cpanelfree", slot)] = ctx.cnt["tQ"]
        @block.scalar
        def _(a, oc=oc):
            ctx.wait(a, "tQ")
            hb = (oc % 2) * 64
            cb = (oc // 2) * TA
            for h in range(2):
                a.activation(K2[hb:hb + 64, cb + h * 512: cb + h * 512 + 512],
                             P[h][0:64, 0:512], AF.Identity,
                             bias=KVBS[0:64, oc:oc + 1])
                act = a.activation(V2[hb:hb + 64, cb + h * 512: cb + h * 512 + 512],
                                   P[h][64:128, 0:512], AF.Identity,
                                   bias=KVBS[64:128, oc:oc + 1])
            ctx.inc(act, "aE")
        M[("cqevac",)] = ctx.cnt["aE"]
    kv_done = ctx.cnt["aE"]

    for jb in range(8):
        for h in range(4):
            pb = h % 2
            @block.tensor
            def _(t, jb=jb, h=h, pb=pb, kv_done=kv_done):
                if jb == 0 and h == 0:
                    t.wait_ge(ctx.sem["aE"], kv_done)
                ctx.mwait(t, "vE", ("csbank", pb))
                hb = (h % 2) * 64
                cq = (h // 2) * 8192
                ck = (h // 2) * TA
                for jx in range(8):
                    jj = jb * 8 + jx
                    mm = t.matmul(
                        P[pb][:, jx * 16:(jx + 1) * 16],
                        QHC[hb:hb + 64, cq + jj * 128: cq + jj * 128 + 128],
                        K2[hb:hb + 64, ck + jj * 16: ck + jj * 16 + 16],
                        start=True, stop=True)
                ctx.inc(mm, "tQ")
            @block.vector
            def _(v, jb=jb, h=h, pb=pb):
                ctx.wait(v, "tQ")
                sap = _ap3p(P[pb], 0, 128, 0, [(8, 16), (16, 1)])
                ctx.inc(v.reduce_max(MXC[:], sap, AX), "vE")
                ctx.wait(v, "vE")
                ctx.inc(v.tensor_tensor(
                    _ap3(SE, 0, 128, 0, [(8, 16), (16, 1)]),
                    sap,
                    MXC[:].unsqueeze(2).broadcast_to([128, 8, 16]),
                    ALU.subtract), "vE")
            M[("csbank", pb)] = ctx.cnt["vE"]
            @block.scalar
            def _(a, h=h):
                ctx.wait(a, "vE")
                ctx.inc(a.activation(EH[:, h * 128:(h + 1) * 128], SE[:],
                                     AF.Exp, scale=0.125), "aE")
            @block.vector
            def _(v, h=h):
                ctx.wait(v, "aE")
                ctx.inc(v.reduce_sum(
                    RSC[:], _ap3(EH, 0, 128, h * 128, [(8, 16), (16, 1)]), AX), "vE")
                ctx.wait(v, "vE")
                ctx.inc(v.reciprocal(RINV[:, h * 8:(h + 1) * 8], RSC[:]), "vE")
        eh_done = ctx.cnt["vE"]

        for jx in range(8):
            jj = jb * 8 + jx
            @block.tensor
            def _(t, jb=jb, jx=jx, jj=jj, eh_done=eh_done):
                if jx == 0:
                    t.wait_ge(ctx.sem["vE"], eh_done)
                ctx.mwait(t, "vE", ("cetb",))
                for h in range(4):
                    mm = t.transpose(PETB[:, h * 128:(h + 1) * 128],
                                     EH[:, h * 128 + jx * 16: h * 128 + jx * 16 + 16],
                                     IDNC[:])
                ctx.inc(mm, "tQ")
                for h in range(4):
                    hb = (h % 2) * 64
                    mm = t.transpose(PVT[h % 2][:, (h // 2) * 64:(h // 2) * 64 + 64],
                                     V2[hb:hb + 64, (h // 2) * TA + jj * 16:
                                        (h // 2) * TA + jj * 16 + 16],
                                     IDNC[hb:hb + 64, hb:hb + 64])
                ctx.inc(mm, "tQ")
            @block.vector
            def _(v, jx=jx):
                ctx.wait(v, "tQ")
                v.tensor_copy(ETS[:], PETB[:])
                v.tensor_copy(VTS[:, 0:64], PVT[0][:, 0:64])
                v.tensor_copy(VTS[:, 64:128], PVT[1][:, 0:64])
                v.tensor_copy(VTS[:, 128:192], PVT[0][:, 64:128])
                ctx.inc(v.tensor_copy(VTS[:, 192:256], PVT[1][:, 64:128]), "vE")
            M[("cetb",)] = ctx.cnt["vE"]
            ets_done = ctx.cnt["vE"]
            @block.tensor
            def _(t, ets_done=ets_done):
                t.wait_ge(ctx.sem["vE"], ets_done)
                ctx.mwait(t, "vE", ("caps",))
                for h in range(4):
                    mm = t.matmul(PAP[:, h * 64:(h + 1) * 64],
                                  ETS[:, h * 128:(h + 1) * 128],
                                  VTS[:, h * 64:(h + 1) * 64],
                                  start=True, stop=True)
                ctx.inc(mm, "tQ")
            @block.vector
            def _(v, jx=jx):
                ctx.wait(v, "tQ")
                for h in range(4):
                    mu = v.tensor_scalar(AOS[:, h * 64:(h + 1) * 64],
                                         PAP[:, h * 64:(h + 1) * 64],
                                         RINV[:, h * 8 + jx: h * 8 + jx + 1],
                                         None, ALU.mult)
                ctx.inc(mu, "vE")
            aos_done = ctx.cnt["vE"]
            M[("caps",)] = aos_done
            @block.tensor
            def _(t, aos_done=aos_done):
                t.wait_ge(ctx.sem["vE"], aos_done)
                ctx.mwait(t, "vE", ("cpot",))
                for h in range(4):
                    mm = t.transpose(POT[:, h * 128:(h + 1) * 128],
                                     AOS[:, h * 64:(h + 1) * 64], IDNC[:])
                ctx.inc(mm, "tQ")
            @block.vector
            def _(v, jj=jj):
                ctx.wait(v, "tQ")
                for h in range(4):
                    hb = (h % 2) * 64
                    mu = v.tensor_copy(
                        _ap3(QOUT, hb, 64, (h // 2) * TA + jj * 16, [(8, 2 * TA), (16, 1)]),
                        _ap3p(POT, 0, 64, h * 128, [(8, 16), (16, 1)]))
                ctx.inc(mu, "vE")
            M[("cpot",)] = ctx.cnt["vE"]
    attn_done = ctx.cnt["vE"]

    for os in range(4):
        @block.gpsimd
        def _(g, os=os):
            ctx.wait(g, "dWO")
            ctx.mwait(g, "tQ", ("cwosfree",))
            for kc in range(16):
                ctx.inc(g.dma_start(
                    out=WOSC[:, kc * 512:(kc + 1) * 512],
                    in_=io["woc"][kc * 128:(kc + 1) * 128, os * 512:(os + 1) * 512]),
                    "dWO", 16)
        wo_cnt = ctx.cnt["dWO"]
        for tc in range(8):
            it = os * 8 + tc
            pp = it % 2
            @block.sync
            def _(s, os=os, tc=tc, it=it):
                ctx.wait(s, "dX0")
                ctx.mwait(s, "vE", ("cx0free",))
                ctx.inc(s.dma_start(
                    out=XCS[:],
                    in_=xvf[l0 + tc * 8: l0 + tc * 8 + 8, :,
                            os * 512:(os + 1) * 512]), "dX0", 16)
            x0_cnt = ctx.cnt["dX0"]
            @block.tensor
            def _(t, os=os, tc=tc, pp=pp, wo_cnt=wo_cnt, it=it, attn_done=attn_done):
                t.wait_ge(ctx.sem["dWO"], wo_cnt)
                if it < 2:
                    t.wait_ge(ctx.sem["vE"], attn_done)
                else:
                    ctx.mwait(t, "vE", ("coprojfree", pp))
                for kc in range(16):
                    mm = t.matmul(P[pp][:, 0:512],
                                  QOUT[:, kc * TA + tc * 128: kc * TA + tc * 128 + 128],
                                  WOSC[:, kc * 512:(kc + 1) * 512],
                                  start=(kc == 0), stop=(kc == 15))
                ctx.inc(mm, "tQ")
            M[("cwosfree",)] = ctx.cnt["tQ"]
            @block.vector
            def _(v, os=os, pp=pp, x0_cnt=x0_cnt):
                ctx.wait(v, "tQ")
                v.wait_ge(ctx.sem["dX0"], x0_cnt)
                ctx.wait(v, "dRS", ctx.marks.get(("crslfree",), 0))
                v.tensor_add(RCS[:], P[pp][:, 0:512], XCS[:])
                ctx.inc(v.tensor_add(RCS[:], RCS[:],
                                     OPBC[:, os * 512:(os + 1) * 512]), "vE")
            M[("coprojfree", pp)] = ctx.cnt["vE"]
            M[("cx0free",)] = ctx.cnt["vE"]
            r_ev = ctx.cnt["vE"]
            @block.scalar
            def _(a, os=os, tc=tc, r_ev=r_ev, l0=l0):
                ctx.wait(a, "dRS")
                a.wait_ge(ctx.sem["vE"], r_ev)
                ctx.inc(a.dma_start(
                    out=r3v[l0 + tc * 8: l0 + tc * 8 + 8, :,
                            os * 512:(os + 1) * 512],
                    in_=RCS[:]), "dRS", 16)
            M[("crslfree",)] = ctx.cnt["dRS"]


def _ap3(tensor, pbase, pcount, off, dims):
    ap = tensor[pbase:pbase + pcount, :]
    import concourse.ap as cap
    new_dims = [list(x) for x in ap.ap]
    free = [[st, cn] for (cn, st) in [(c, s) for (c, s) in dims]]
    free = [[s, c] for (c, s) in dims]
    from dataclasses import replace
    base = tensor[pbase:pbase + pcount, off:off + 1]
    newap = [list(base.ap[0])] + free
    return cap.AP(base.tensor, base.offset, newap)


def _ap3p(tensor, pbase, pcount, off, dims):
    return _ap3(tensor, pbase, pcount, off, dims)


def _prep_c(inp):
    d = {}
    d["wqc_cp"] = _cp_layout(np.ascontiguousarray(inp["L1_attn_q_proj"].T))
    d["wkvc_cp"] = _cp_layout(np.ascontiguousarray(inp["L1_attn_kv_proj"].T))
    d["woc"] = np.ascontiguousarray(inp["L1_attn_out_proj"].T)
    d["qbc_r"] = np.ascontiguousarray(inp["L1_attn_q_bias"].reshape(16, 128).T)
    d["kvbc_r"] = np.ascontiguousarray(inp["L1_attn_kv_bias"].reshape(4, 128).T)
    d["obc_row"] = np.ascontiguousarray(inp["L1_attn_out_bias"].reshape(1, D))
    return d


def emit_phase_d(ctx, io):
    nc, block = ctx.nc, ctx.block
    TS_ = 512
    NST = NT // TS_

    al = nc.alloc_sbuf_tensor
    XFD = al("d_xf", [128, 16 * TS_], BF16)
    X1A = al("d_x1a", [128, 16 * TS_], BF16)
    X1B = al("d_x1b", [128, 16 * TS_], BF16)
    X2B = XFD
    PRD = al("d_prd", [128, 64 * TS_], BF16)
    ZB = PRD
    XT3 = al("d_xt3", [128, 4 * D], F32)
    WUP = [al(f"d_wup{i}", [128, D], BF16) for i in range(2)]
    WGT = [al(f"d_wgt{i}", [128, D], BF16) for i in range(2)]
    WDN = [al(f"d_wdn{i}", [128, D], BF16) for i in range(2)]
    FD = al("d_fd", [64, D], BF16)
    FE = al("d_fe", [64, D], BF16)
    H128 = al("d_h128", [128, 128], BF16)
    H64C = al("d_h64", [64, 64], BF16)
    SUPS = al("d_sups", [128, 64], F32)
    SGS = al("d_sgs", [128, 64], F32)
    SUU = al("d_suu", [128, 16], F32)
    SDB = al("d_sdb", [128, D], F32)
    UEV = al("d_uev", [128, TS_], F32)
    GEV = al("d_gev", [128, TS_], F32)
    R4S = al("d_r4s", [128, 512], F32)
    ONED = al("d_one", [1, 128], F32)
    WRD = al("d_wrd", [1, 512], F32)

    P = [nc.alloc_psum_tensor(f"d_p{i}", [128, 512], F32) for i in range(8)]

    sbD = dict(R1=al("d_r1", [128, D], F32), XLB=al("d_xlb", [128, D], BF16),
               STT=al("d_stt", [128, 8], F32), EPS=al("d_eps", [128, 1], F32))

    @block.vector
    def _(v):
        v.memset(sbD["EPS"][:], 1e-5)
        ctx.inc(v.memset(ONED[:], 1.0), "vE")

    @block.sync
    def _(s):
        ctx.wait(s, "dC")
        for dst, srcn in [(H128, "c_h128"), (H64C, "c_h64"), (SUPS, "wsu_r"),
                          (SGS, "wsg_r"), (SGS, "wsg_r"), (SUU, "suug_r")]:
            ctx.inc(s.dma_start(out=dst[:], in_=io[srcn]), "dC", 16)
        ctx.inc(s.dma_start(out=UEV[:, 0:64], in_=io["sudp_r"]), "dC", 16)

    @block.vector
    def _(v):
        ctx.wait(v, "dC")
        ctx.wait(v, "vE")
        rt = 1.0 / float(np.sqrt(2048.0))
        v.scalar_tensor_tensor(SUPS[:], SUPS[:], rt, UEV[:, 0:64], ALU.mult, ALU.mult)
        ctx.inc(v.tensor_scalar(SGS[:], SGS[:], rt, None, ALU.mult), "vE")
    svec_done = ctx.cnt["vE"]

    for os in range(4):
        @block.sync
        def _(s, os=os):
            ctx.wait(s, "dC")
            if os > 0:
                ctx.wait(s, "tQ")
            ctx.inc(s.dma_start(out=WRD[:],
                                in_=io["wsdp_row"][0:1, os * 512:(os + 1) * 512]), "dC", 16)
        @block.tensor
        def _(t, os=os):
            ctx.wait(t, "dC")
            ctx.wait(t, "vE")
            if os == 0:
                ctx.wait(t, "tQ")
            ctx.inc(t.matmul(P[0][:, 0:512], ONED[0:1, :], WRD[0:1, 0:512],
                             start=True, stop=True), "tQ")
        @block.vector
        def _(v, os=os):
            ctx.wait(v, "tQ")
            ctx.inc(v.tensor_scalar(SDB[:, os * 512:(os + 1) * 512], P[0][:, 0:512],
                                    1.0 / float(np.sqrt(8192.0)), None, ALU.mult), "vE")
    sdb_done = ctx.cnt["vE"]

    @block.gpsimd
    def _(g):
        ctx.wait(g, "dXB")
        for i in range(8):
            ctx.inc(g.dma_start(out=io["wupb"][i * 1024:(i + 1) * 1024, :],
                                in_=io["wup_cp"][i * 1024:(i + 1) * 1024, :]), "dXB", 16)
            ctx.inc(g.dma_start(out=io["wgateb"][i * 1024:(i + 1) * 1024, :],
                                in_=io["wgate_cp"][i * 1024:(i + 1) * 1024, :]), "dXB", 16)
    precast_done = ctx.cnt["dXB"]

    wdv = io["wdownT"].rearrange("(a p) d -> a p d", p=128)
    w2v = io["w2db"].rearrange("(a p) d -> a p d", p=128)
    for p in range(128):
        @block.gpsimd
        def _(g, p=p):
            ctx.wait(g, "dP0")
            ctx.mwait(g, "tQ", ("dfoldmm",))
            ctx.inc(g.dma_start(out=FD[:], in_=wdv[:, p, :]), "dP0", 16)
        fcnt = ctx.cnt["dP0"]
        @block.tensor
        def _(t, p=p, fcnt=fcnt):
            t.wait_ge(ctx.sem["dP0"], fcnt)
            ctx.mwait(t, "vE", ("dfoldev",))
            if p == 0:
                ctx.wait(t, "tQ")
            for q4 in range(4):
                bank = P[1] if q4 < 2 else P[2]
                qb = (q4 % 2) * 64
                mm = t.matmul(bank[qb:qb + 64, 0:512], H64C[:],
                              FD[:, q4 * 512:(q4 + 1) * 512], start=True, stop=True)
            ctx.inc(mm, "tQ")
        ctx.marks[("dfoldmm",)] = ctx.cnt["tQ"]
        @block.vector
        def _(v, p=p):
            ctx.wait(v, "tQ")
            ctx.wait(v, "dLS", ctx.marks.get(("dfoldst",), 0))
            for q4 in range(4):
                bank = P[1] if q4 < 2 else P[2]
                qb = (q4 % 2) * 64
                mu = v.tensor_copy(FE[:, q4 * 512:(q4 + 1) * 512],
                                   bank[qb:qb + 64, 0:512])
            ctx.inc(mu, "vE")
        ctx.marks[("dfoldev",)] = ctx.cnt["vE"]
        fe_done = ctx.cnt["vE"]
        @block.scalar
        def _(a, p=p, fe_done=fe_done):
            ctx.wait(a, "dLS")
            a.wait_ge(ctx.sem["vE"], fe_done)
            ctx.inc(a.dma_start(out=w2v[:, p, :], in_=FE[:]), "dLS", 16)
        ctx.marks[("dfoldst",)] = ctx.cnt["dLS"]

    for st in range(NST):
        _emit_d_subtile(ctx, io, st, TS_, XFD, X1A, X1B, X2B, PRD, ZB, XT3,
                        WUP, WGT, WDN, H128, SUPS, SGS, SUU, SDB, UEV, GEV,
                        R4S, P, svec_done, sdb_done, precast_done)
        _emit_lnf(ctx, sbD, io["r4D"], io["xout"], None,
                  [st * TS_ + k * 128 for k in range(4)])


def _emit_d_subtile(ctx, io, st, TS_, XFD, X1A, X1B, X2B, PRD, ZB, XT3,
                    WUP, WGT, WDN, H128, SUPS, SGS, SUU, SDB, UEV, GEV,
                    R4S, P, svec_done, sdb_done, precast_done):
    nc, block = ctx.nc, ctx.block
    t0 = st * TS_
    M = ctx.marks

    @block.sync
    def _(s):
        ctx.wait(s, "dXF")
        if st > 0:
            ctx.wait(s, "tQ")
            ctx.wait(s, "vE")
        for c in range(16):
            ctx.inc(s.dma_start_transpose(
                XFD[:, c * TS_:(c + 1) * TS_],
                io["xln3b"][t0:t0 + TS_, c * 128:(c + 1) * 128]), "dXF", 16)
    xf_cnt = ctx.cnt["dXF"]
    @block.sync
    def _(s):
        ctx.wait(s, "dX0")
        ctx.mwait(s, "vE", ("dxt3free",))
        for tc in range(4):
            ctx.inc(s.dma_start(out=XT3[:, tc * D:(tc + 1) * D],
                                in_=io["xln3f"][t0 + tc * 128: t0 + tc * 128 + 128, :]),
                    "dX0", 16)
    xt_cnt = ctx.cnt["dX0"]

    @block.vector
    def _(v):
        v.wait_ge(ctx.sem["dXF"], xf_cnt)
        v.wait_ge(ctx.sem["vE"], svec_done)
        ctx.wait(v, "tQ")
        for c in range(16):
            mu = v.tensor_scalar(X1A[:, c * TS_:(c + 1) * TS_],
                                 XFD[:, c * TS_:(c + 1) * TS_],
                                 SUU[:, c:c + 1], None, ALU.mult)
        ctx.inc(mu, "vE")

    bufs = [X1A, X1B]
    cur = 0
    for k in range(4):
        dist = 1 << k
        src_, dst_ = bufs[cur], bufs[1 - cur]
        @block.vector
        def _(v, k=k, dist=dist, src_=src_, dst_=dst_):
            ctx.wait(v, "vE")
            for c in range(16):
                if (c >> k) & 1 == 0:
                    pa = src_[:, c * TS_:(c + 1) * TS_]
                    pb_ = src_[:, (c + dist) * TS_:(c + dist + 1) * TS_]
                    v.tensor_add(dst_[:, c * TS_:(c + 1) * TS_], pa, pb_)
                    mu = v.tensor_sub(dst_[:, (c + dist) * TS_:(c + dist + 1) * TS_],
                                      pa, pb_)
            ctx.inc(mu, "vE")
        cur = 1 - cur
    x1_buf = bufs[cur]
    x1_done = ctx.cnt["vE"]

    for c in range(16):
        pb = c % 2
        @block.tensor
        def _(t, c=c, pb=pb, x1_done=x1_done):
            if c == 0:
                t.wait_ge(ctx.sem["vE"], x1_done)
            ctx.mwait(t, "aE", ("dx2ev", pb))
            ctx.inc(t.matmul(P[pb][:, 0:512], H128[:],
                             x1_buf[:, c * TS_:(c + 1) * TS_],
                             start=True, stop=True), "tQ")
        @block.scalar
        def _(a, c=c, pb=pb):
            ctx.wait(a, "tQ")
            ctx.inc(a.activation(X2B[:, c * TS_:(c + 1) * TS_], P[pb][:, 0:512],
                                 AF.Identity), "aE")
        M[("dx2ev", pb)] = ctx.cnt["aE"]
    x2_done = ctx.cnt["aE"]

    for c in range(64):
        slot = c % 2
        psem = "dP0" if slot == 0 else "dP1"
        @block.gpsimd
        def _(g, c=c, slot=slot, psem=psem):
            ctx.wait(g, psem)
            if st == 0 and c < 2:
                g.wait_ge(ctx.sem["dXB"], precast_done)
            ctx.mwait(g, "tQ", ("dugfree", slot))
            ctx.inc(g.dma_start(out=WUP[slot][:],
                                in_=io["wupb"][c * 128:(c + 1) * 128, :]), psem, 16)
            ctx.inc(g.dma_start(out=WGT[slot][:],
                                in_=io["wgateb"][c * 128:(c + 1) * 128, :]), psem, 16)
        pcnt = ctx.cnt[psem]
        @block.tensor
        def _(t, c=c, slot=slot, psem=psem, pcnt=pcnt, x2_done=x2_done):
            t.wait_ge(ctx.sem[psem], pcnt)
            if c == 0:
                t.wait_ge(ctx.sem["aE"], x2_done)
            ctx.mwait(t, "aE", ("dugev", c % 2))
            for kc in range(16):
                t.matmul(P[2 + (c % 2)][:, 0:512], WUP[slot][:, kc * 128:(kc + 1) * 128],
                         X2B[:, kc * TS_:(kc + 1) * TS_],
                         start=(kc == 0), stop=(kc == 15))
            for kc in range(16):
                mm = t.matmul(P[4 + (c % 2)][:, 0:512],
                              WGT[slot][:, kc * 128:(kc + 1) * 128],
                              X2B[:, kc * TS_:(kc + 1) * TS_],
                              start=(kc == 0), stop=(kc == 15))
            ctx.inc(mm, "tQ")
        M[("dugfree", slot)] = ctx.cnt["tQ"]
        @block.scalar
        def _(a, c=c):
            ctx.wait(a, "tQ")
            a.activation(UEV[:], P[2 + (c % 2)][:, 0:512], AF.Identity,
                         scale=SUPS[:, c:c + 1])
            ctx.inc(a.activation(GEV[:], P[4 + (c % 2)][:, 0:512], AF.Silu,
                                 scale=SGS[:, c:c + 1]), "aE")
        M[("dugev", c % 2)] = ctx.cnt["aE"]
        @block.vector
        def _(v, c=c):
            ctx.wait(v, "aE")
            ctx.inc(v.tensor_mul(PRD[:, c * TS_:(c + 1) * TS_], UEV[:], GEV[:]), "vE")
    prd_done = ctx.cnt["vE"]

    for a_ in range(64):
        pb = a_ % 2
        @block.tensor
        def _(t, a_=a_, pb=pb, prd_done=prd_done):
            if a_ == 0:
                t.wait_ge(ctx.sem["vE"], prd_done)
            ctx.mwait(t, "aE", ("dzev", pb))
            ctx.inc(t.matmul(P[pb][:, 0:512], H128[:],
                             PRD[:, a_ * TS_:(a_ + 1) * TS_],
                             start=True, stop=True), "tQ")
        @block.scalar
        def _(a, a_=a_, pb=pb):
            ctx.wait(a, "tQ")
            ctx.inc(a.activation(ZB[:, a_ * TS_:(a_ + 1) * TS_], P[pb][:, 0:512],
                                 AF.Identity), "aE")
        M[("dzev", pb)] = ctx.cnt["aE"]
    z_done = ctx.cnt["aE"]

    for th in range(2):
        for a_ in range(64):
            slot = a_ % 2
            psem = "dP0" if slot == 0 else "dP1"
            @block.gpsimd
            def _(g, a_=a_, slot=slot, psem=psem):
                ctx.wait(g, psem)
                ctx.mwait(g, "tQ", ("ddnfree", slot))
                ctx.inc(g.dma_start(out=WDN[slot][:],
                                    in_=io["w2db"][a_ * 128:(a_ + 1) * 128, :]), psem, 16)
            pcnt = ctx.cnt[psem]
            @block.tensor
            def _(t, th=th, a_=a_, slot=slot, psem=psem, pcnt=pcnt, z_done=z_done):
                t.wait_ge(ctx.sem[psem], pcnt)
                if th == 0 and a_ == 0:
                    t.wait_ge(ctx.sem["aE"], z_done)
                if a_ == 0:
                    ctx.mwait(t, "vE", ("ddnev",))
                for g8 in range(8):
                    tc = th * 2 + g8 // 4
                    os = g8 % 4
                    mm = t.matmul(P[g8][:, 0:512],
                                  ZB[:, a_ * TS_ + tc * 128: a_ * TS_ + tc * 128 + 128],
                                  WDN[slot][:, os * 512:(os + 1) * 512],
                                  start=(a_ == 0), stop=(a_ == 63))
                ctx.inc(mm, "tQ")
            M[("ddnfree", slot)] = ctx.cnt["tQ"]
        dn_mm = ctx.cnt["tQ"]
        for g8 in range(8):
            tc = th * 2 + g8 // 4
            os = g8 % 4
            @block.vector
            def _(v, g8=g8, tc=tc, os=os, dn_mm=dn_mm, xt_cnt=xt_cnt):
                v.wait_ge(ctx.sem["tQ"], dn_mm)
                v.wait_ge(ctx.sem["dX0"], xt_cnt)
                ctx.wait(v, "dRS", ctx.marks.get(("drslfree",), 0))
                v.tensor_mul(R4S[:], P[g8][:, 0:512], SDB[:, os * 512:(os + 1) * 512])
                ctx.inc(v.tensor_add(R4S[:], R4S[:],
                                     XT3[:, tc * D + os * 512: tc * D + os * 512 + 512]),
                        "vE")
            r_ev = ctx.cnt["vE"]
            @block.scalar
            def _(a, tc=tc, os=os, r_ev=r_ev):
                ctx.wait(a, "dRS")
                a.wait_ge(ctx.sem["vE"], r_ev)
                ctx.inc(a.dma_start(
                    out=io["r4D"][t0 + tc * 128: t0 + tc * 128 + 128,
                                  os * 512:(os + 1) * 512], in_=R4S[:]), "dRS", 16)
            ctx.marks[("drslfree",)] = ctx.cnt["dRS"]
        ctx.marks[("ddnev",)] = ctx.cnt["vE"]
    ctx.marks[("dxt3free",)] = ctx.cnt["vE"]


def _prep_d(inp):
    d = {}
    d["wup_cp"] = _cp_layout(np.ascontiguousarray(inp["L1_ffn_W_up"].T))
    d["wgate_cp"] = _cp_layout(np.ascontiguousarray(inp["L1_ffn_W_gate"].T))
    d["wdownT"] = np.ascontiguousarray(inp["L1_ffn_W_down"].T)
    ws = inp["L1_ffn_Wscale_ug"]
    d["wsu_r"] = np.ascontiguousarray(ws[:8192].reshape(64, 128).T)
    d["wsg_r"] = np.ascontiguousarray(ws[8192:].reshape(64, 128).T)
    d["sudp_r"] = np.ascontiguousarray(inp["L1_ffn_SU_dp"].reshape(64, 128).T)
    d["suug_r"] = np.ascontiguousarray(inp["L1_ffn_SU_ug"].reshape(16, 128).T)
    d["wsdp_row"] = np.ascontiguousarray(inp["L1_ffn_Wscale_dp"].reshape(1, D))
    d["c_h128"] = _hadamard(128).astype(BF16NP)
    d["c_h64"] = _hadamard(64).astype(BF16NP)
    return d


def _hadamard(n):
    h = np.array([[1.0]], dtype=np.float32)
    while h.shape[0] < n:
        h = np.block([[h, h], [h, -h]])
    return h
